# version 01
# baseline (speedup 1.0000x reference)
import numpy as np
from contextlib import ExitStack

import concourse.bass as bass
import concourse.mybir as mybir
from concourse.bass_utils import run_bass_kernel_spmd
from concourse.aot_env import allow_ici_all_to_all

F32 = mybir.dt.float32
BF16 = mybir.dt.bfloat16
AL = mybir.AluOpType
AF = mybir.ActivationFunctionType
AX = mybir.AxisListType.X

TC = 2048
DM = 1024
NF = 8
NE = 8
NH = 2
HD = 64
CS = 128
NG = 32
GT = 512
GC = 4
NTQ = 4
TQ = 512
NC_ = 8

ENGINES = ("sync", "gpsimd", "tensor", "vector", "scalar")


class Sched:
    def __init__(self, nc, ctx, sem_names):
        self.nc = nc
        self.sems = {n: ctx.enter_context(nc.semaphore(n)) for n in sem_names}
        self.cnt = {n: 0 for n in sem_names}
        self.prog = {e: [] for e in ENGINES}

    def emit(self, eng, fn, inc=None, inc_by=1, waits=()):
        wl = []
        for s in waits:
            if s is None:
                continue
            wl.append((s, self.cnt[s]) if isinstance(s, str) else s)
        if inc is not None:
            self.cnt[inc] += inc_by
        self.prog[eng].append((fn, wl, inc, inc_by))

    def now(self, sem):
        return (sem, self.cnt[sem])

    def replay(self, eng_name, eng):
        seen = {}
        for fn, wl, inc, inc_by in self.prog[eng_name]:
            for s, c in wl:
                if c > 0 and seen.get(s, -1) < c:
                    eng.wait_ge(self.sems[s], c)
                    seen[s] = c
            inst = fn(eng)
            if inc is not None:
                inst.then_inc(self.sems[inc], inc_by)


def build_nc(debug=False):
    nc = bass.Bass("TRN2", target_bir_lowering=False)
    ctx = ExitStack()

    xT = nc.dram_tensor("xT", [DM, TC], F32, kind="ExternalInput")
    wqd = nc.dram_tensor("wqd", [DM, 128], F32, kind="ExternalInput")
    wkd = nc.dram_tensor("wkd", [DM, 128], F32, kind="ExternalInput")
    wvd = nc.dram_tensor("wvd", [DM, 128], F32, kind="ExternalInput")
    wod = nc.dram_tensor("wod", [DM, DM], F32, kind="ExternalInput")
    wet = nc.dram_tensor("wet", [8, 128, NE, 8, 128], F32, kind="ExternalInput")
    bed = nc.dram_tensor("bed", [NE, DM], F32, kind="ExternalInput")
    wgd = nc.dram_tensor("wgd", [DM, NE], F32, kind="ExternalInput")
    bg128 = nc.dram_tensor("bg128", [128, NE], F32, kind="ExternalInput")
    araw = nc.dram_tensor("araw", [NH, 1], F32, kind="ExternalInput")
    braw = nc.dram_tensor("braw", [NH, 1], F32, kind="ExternalInput")
    out = nc.dram_tensor("out", [DM, TC], F32, kind="ExternalOutput")

    ag_in = nc.dram_tensor("ag_in", [DM, TC], BF16)
    ag_out = nc.dram_tensor("ag_out", [NC_, DM, TC], BF16)
    a2a_in = nc.dram_tensor("a2a_in", [NC_, 128, TC], BF16)
    a2a_out = nc.dram_tensor("a2a_out", [NC_, 128, TC], BF16)

    dbg = {}
    if debug:
        for nm, shp in [("d_x1", [DM, TC]), ("d_xhat", [DM, TC]),
                        ("d_y", [128, TC]), ("d_x2", [DM, TC]),
                        ("d_srows", [NE, TC]), ("d_kf", [128, GT]),
                        ("d_gc", [128, 6, 16]), ("d_gL", [128, 16, NE]),
                        ("d_graw", [128, 16, NE]), ("d_gfix", [128, 16, NE]),
                        ("d_m1s", [128, 16]), ("d_rbf", [1, TC]),
                        ("d_seq", [128, 16, 16]), ("d_N", [128, GC, 128]),
                        ("d_X0", [128, GC, 128]), ("d_ab", [NH, 8]),
                        ("d_vs", [128, 1]),
                        ("d_kt", [128, GC, 128]), ("d_X", [128, GC, 128])]:
            dbg[nm] = nc.dram_tensor(nm, shp, F32, kind="ExternalOutput")

    x_sb = ctx.enter_context(nc.sbuf_tensor("x_sb", [128, NF, TC], F32))
    xc = ctx.enter_context(nc.sbuf_tensor("xc", [128, NF, TC], BF16))
    mrow = ctx.enter_context(nc.sbuf_tensor("mrow", [1, TC], BF16))
    rrow = ctx.enter_context(nc.sbuf_tensor("rrow", [1, TC], BF16))
    Rb = ctx.enter_context(nc.sbuf_tensor("Rb", [128, TC], BF16))
    Mb = ctx.enter_context(nc.sbuf_tensor("Mb", [128, TC], BF16))
    srows = ctx.enter_context(nc.sbuf_tensor("srows", [NE, TC], BF16))
    g_bf = ctx.enter_context(nc.sbuf_tensor("g_bf", [NE, TQ], BF16))
    gL = ctx.enter_context(nc.sbuf_tensor("gL", [128, 16, NE], F32))
    gw1 = ctx.enter_context(nc.sbuf_tensor("gw1", [128, 16, NE], F32))
    gcols = ctx.enter_context(nc.sbuf_tensor("gcols", [128, 6, 16], F32))
    stm = ctx.enter_context(nc.sbuf_tensor("stm", [128, 16, NE], BF16))
    wtm = ctx.enter_context(nc.sbuf_tensor("wtm", [128, 16, NE], BF16))
    IDN = ctx.enter_context(nc.sbuf_tensor("IDN", [128, 128], BF16))
    MLOW = ctx.enter_context(nc.sbuf_tensor("MLOW", [128, 128], BF16))
    MUP = ctx.enter_context(nc.sbuf_tensor("MUP", [128, 128], BF16))
    ONER = ctx.enter_context(nc.sbuf_tensor("ONER", [1, 128], BF16))
    ONEC = ctx.enter_context(nc.sbuf_tensor("ONEC", [128, 1], BF16))
    HSEL = ctx.enter_context(nc.sbuf_tensor("HSEL", [2, 128], BF16))
    BONES = ctx.enter_context(nc.sbuf_tensor("BONES", [128, 2], BF16))
    ESEL = ctx.enter_context(nc.sbuf_tensor("ESEL", [NE, NE, 128], BF16))
    wq_bf = ctx.enter_context(nc.sbuf_tensor("wq_bf", [128, NF, 128], BF16))
    wk_bf = ctx.enter_context(nc.sbuf_tensor("wk_bf", [128, NF, 128], BF16))
    wv_bf = ctx.enter_context(nc.sbuf_tensor("wv_bf", [128, NF, 128], BF16))
    wg_bf = ctx.enter_context(nc.sbuf_tensor("wg_bf", [128, NF, NE], BF16))
    be_bf = ctx.enter_context(nc.sbuf_tensor("be_bf", [NE, DM], BF16))
    bg_sb = ctx.enter_context(nc.sbuf_tensor("bg_sb", [128, NE], F32))
    ab_sb = ctx.enter_context(nc.sbuf_tensor("ab_sb", [NH, 8], F32))
    graw_dbg = ctx.enter_context(nc.sbuf_tensor("graw_dbg", [128, 16, NE], BF16)) if debug else None
    gfix_dbg = ctx.enter_context(nc.sbuf_tensor("gfix_dbg", [128, 16, NE], BF16)) if debug else None
    m1_dbg = ctx.enter_context(nc.sbuf_tensor("m1_dbg", [128, 16], F32)) if debug else None
    seq_dbg = ctx.enter_context(nc.sbuf_tensor("seq_dbg", [128, 16, 16], BF16)) if debug else None
    SEQI = [0]
    vscale = ctx.enter_context(nc.sbuf_tensor("vscale", [128, 4], F32))
    eps_sb = ctx.enter_context(nc.sbuf_tensor("eps_sb", [1, 1], F32))
    mrow_s = ctx.enter_context(nc.sbuf_tensor("mrow_s", [1, TQ], BF16))

    pbA = ctx.enter_context(nc.psum_tensor("pbA", [128, 512], F32))
    pbB = ctx.enter_context(nc.psum_tensor("pbB", [128, 512], F32))
    pbC = ctx.enter_context(nc.psum_tensor("pbC", [128, 512], F32))
    pbS = ctx.enter_context(nc.psum_tensor("pbS", [128, 512], F32))
    pbP = ctx.enter_context(nc.psum_tensor("pbP", [128, 512], F32))
    pbPT = ctx.enter_context(nc.psum_tensor("pbPT", [128, 512], F32))
    pbX = ctx.enter_context(nc.psum_tensor("pbX", [128, 512], F32))
    pbY = ctx.enter_context(nc.psum_tensor("pbY", [128, 512], F32))

    S = Sched(nc, ctx, ["sin", "swe", "sxg", "sst", "scc", "t", "v", "a", "g"])

    def MM(out_ap, lhsT, rhs, start=True, stop=True):
        return lambda e: e.matmul(out_ap, lhsT, rhs, start=start, stop=stop)

    S.emit("sync", lambda e: e.dma_start(
        out=x_sb[:, :, :],
        in_=bass.AP(xT, 0, [[TC, 128], [128 * TC, NF], [1, TC]])),
        inc="sin", inc_by=16)
    for td, hb in ((wqd, wq_bf), (wkd, wk_bf), (wvd, wv_bf)):
        S.emit("gpsimd", lambda e, td=td, hb=hb: e.dma_start(
            out=hb[:, :, :],
            in_=bass.AP(td, 0, [[128, 128], [128 * 128, NF], [1, 128]])),
            inc="sin", inc_by=16)
    S.emit("gpsimd", lambda e: e.dma_start(
        out=wg_bf[:, :, :],
        in_=bass.AP(wgd, 0, [[NE, 128], [128 * NE, NF], [1, NE]])),
        inc="sin", inc_by=16)
    S.emit("gpsimd", lambda e: e.dma_start(out=be_bf[:, :], in_=bed.ap()),
           inc="sin", inc_by=16)
    S.emit("sync", lambda e: e.dma_start(out=bg_sb[:, :], in_=bg128.ap()),
           inc="sin", inc_by=16)
    S.emit("sync", lambda e: e.dma_start(out=ab_sb[:, 0:1], in_=araw.ap()),
           inc="sin", inc_by=16)
    S.emit("sync", lambda e: e.dma_start(out=ab_sb[:, 1:2], in_=braw.ap()),
           inc="sin", inc_by=16)
    lin = S.now("sin")

    def build_consts(e):
        e.memset(eps_sb[:, :], 1e-5)
        e.memset(MLOW[:, :], 1.0)
        e.affine_select(out=MLOW[:, :], in_=MLOW[:, :], pattern=[[-1, 128]],
                        compare_op=AL.is_ge, fill=0.0, base=-1, channel_multiplier=1)
        e.memset(MUP[:, :], 1.0)
        e.affine_select(out=MUP[:, :], in_=MUP[:, :], pattern=[[1, 128]],
                        compare_op=AL.is_ge, fill=0.0, base=-1, channel_multiplier=-1)
        e.memset(IDN[:, :], 1.0)
        e.affine_select(out=IDN[:, :], in_=IDN[:, :], pattern=[[-1, 128]],
                        compare_op=AL.is_equal, fill=0.0, base=0, channel_multiplier=1)
        e.memset(ONER[:, :], 1.0)
        e.memset(ONEC[:, :], 1.0)
        e.memset(HSEL[:, :], 1.0)
        e.affine_select(out=HSEL[:, :], in_=HSEL[:, :], pattern=[[-1, 128]],
                        compare_op=AL.is_ge, fill=0.0, base=63, channel_multiplier=64)
        e.affine_select(out=HSEL[:, :], in_=HSEL[:, :], pattern=[[1, 128]],
                        compare_op=AL.is_ge, fill=0.0, base=0, channel_multiplier=-64)
        e.memset(ESEL[:, :, :], 1.0)
        e.affine_select(out=ESEL[:, :, :], in_=ESEL[:, :, :],
                        pattern=[[-1, NE], [0, 128]],
                        compare_op=AL.is_equal, fill=0.0, base=0,
                        channel_multiplier=1)
        e.memset(BONES[:, :], 1.0)
        e.affine_select(out=BONES[:, :], in_=BONES[:, :], pattern=[[-64, 2]],
                        compare_op=AL.is_ge, fill=0.0, base=0, channel_multiplier=1)
        return e.affine_select(out=BONES[:, :], in_=BONES[:, :], pattern=[[64, 2]],
                               compare_op=AL.is_ge, fill=0.0, base=63,
                               channel_multiplier=-1)

    S.emit("gpsimd", build_consts, inc="g", waits=[lin])

    S.emit("scalar", lambda e: e.activation(out=ab_sb[:, 2:3], in_=ab_sb[:, 0:1],
                                            func=AF.Sigmoid, scale=1.0),
           inc="a", waits=[lin])
    S.emit("scalar", lambda e: e.activation(out=ab_sb[:, 3:4], in_=ab_sb[:, 1:2],
                                            func=AF.Sigmoid, scale=1.0), inc="a")
    S.emit("vector", lambda e: e.tensor_scalar(
        out=ab_sb[:, 2:4], in0=ab_sb[:, 2:4], scalar1=0.5, scalar2=None,
        op0=AL.mult), inc="v", waits=["a"])
    S.emit("scalar", lambda e: e.activation(out=ab_sb[:, 6:7], in_=ab_sb[:, 2:3],
                                            func=AF.Ln, scale=1.0),
           inc="a", waits=["v"])
    S.emit("scalar", lambda e: e.activation(out=ab_sb[:, 4:5], in_=ab_sb[:, 6:7],
                                            func=AF.Exp, scale=0.5), inc="a")
    S.emit("scalar", lambda e: e.activation(out=ab_sb[:, 5:6], in_=ab_sb[:, 6:7],
                                            func=AF.Exp, scale=-0.5), inc="a")
    S.emit("vector", lambda e: e.tensor_tensor(out=ab_sb[:, 7:8], in0=ab_sb[:, 5:6],
                                               in1=ab_sb[:, 3:4], op=AL.mult),
           inc="v", waits=["a"])


    def emit_ln(xsq_h, xsq_pitch):
        S.emit("vector", lambda e: e.tensor_copy(out=xc[:, :, :], in_=x_sb[:, :, :]),
               inc="v")
        xsq_full = bass.AP(xsq_h, 0, [[xsq_pitch, 128], [TC, NF], [1, TC]])
        S.emit("vector", lambda e: e.tensor_tensor(
            out=xsq_full, in0=xc[:, :, :], in1=xc[:, :, :], op=AL.mult), inc="v")
        vsq = S.now("v")
        for tq in range(NTQ):
            sl = slice(tq * TQ, (tq + 1) * TQ)
            for f in range(NF):
                S.emit("tensor", MM(pbC[0:1, 0:TQ], ONEC[:, :], xc[:, f, sl],
                                    start=(f == 0), stop=(f == NF - 1)),
                       waits=([vsq, "v", "a"] if f == 0 else []))
            for f in range(NF):
                rhs = bass.AP(xsq_h, f * TC + tq * TQ, [[xsq_pitch, 128], [1, TQ]])
                S.emit("tensor", MM(pbS[0:1, 0:TQ], ONEC[:, :], rhs,
                                    start=(f == 0), stop=(f == NF - 1)),
                       inc=("t" if f == NF - 1 else None))
            S.emit("vector", lambda e, sl=sl: e.tensor_scalar(
                out=mrow[0:1, sl], in0=pbC[0:1, 0:TQ], scalar1=1.0 / DM,
                scalar2=None, op0=AL.mult), inc="v", waits=["t"])
            S.emit("vector", lambda e, sl=sl: e.tensor_tensor(
                out=mrow_s[0:1, :], in0=mrow[0:1, sl], in1=mrow[0:1, sl],
                op=AL.mult), inc="v")
            S.emit("vector", lambda e: e.scalar_tensor_tensor(
                out=pbC[0:1, 0:TQ], in0=pbS[0:1, 0:TQ], scalar=1.0 / DM,
                in1=mrow_s[0:1, :], op0=AL.mult, op1=AL.subtract), inc="v")
            S.emit("scalar", lambda e: e.activation(
                out=pbC[0:1, 0:TQ], in_=pbC[0:1, 0:TQ], func=AF.Ln,
                bias=eps_sb[0:1, 0:1], scale=1.0), inc="a", waits=["v", "g"])
            S.emit("scalar", lambda e, sl=sl: e.activation(
                out=rrow[0:1, sl], in_=pbC[0:1, 0:TQ], func=AF.Exp, scale=-0.5),
                inc="a")
        vr = S.now("a")
        for tq in range(NTQ):
            sl = slice(tq * TQ, (tq + 1) * TQ)
            S.emit("tensor", MM(pbC[:, 0:TQ], ONER[:, :], mrow[0:1, sl]),
                   inc="t", waits=[vr, "a", "v"])
            S.emit("scalar", lambda e, sl=sl: e.copy(out=Mb[:, sl],
                                                     in_=pbC[:, 0:TQ]),
                   inc="a", waits=["t"])
            S.emit("tensor", MM(pbC[:, 0:TQ], ONER[:, :], rrow[0:1, sl]),
                   inc="t", waits=["a"])
            S.emit("scalar", lambda e, sl=sl: e.copy(
                out=Rb[:, sl], in_=pbC[:, 0:TQ]), inc="a", waits=["t"])
        S.emit("vector", lambda e: e.tensor_tensor(
            out=xc[:, :, :], in0=xc[:, :, :],
            in1=bass.AP(Mb, 0, [[TC, 128], [0, NF], [1, TC]]),
            op=AL.subtract), inc="v", waits=["a"])

    def emit_gating():
        vxc = S.now("v")
        for tq in range(NTQ):
            sl = slice(tq * TQ, (tq + 1) * TQ)
            for f in range(NF):
                S.emit("tensor", MM(pbC[0:NE, 0:TQ], wg_bf[:, f, :], xc[:, f, sl],
                                    start=(f == 0), stop=(f == NF - 1)),
                       inc=("t" if f == NF - 1 else None),
                       waits=([vxc, "v"] if f == 0 else []))
            S.emit("vector", lambda e: e.tensor_copy(
                out=g_bf[:, :], in_=pbC[0:NE, 0:TQ]), inc="v", waits=["t"])
            for jj in range(4):
                j = tq * 4 + jj
                S.emit("tensor", MM(pbS[:, j * NE:(j + 1) * NE],
                                    g_bf[:, jj * 128:(jj + 1) * 128],
                                    IDN[0:NE, 0:NE]),
                       inc=("t" if jj == 3 else None),
                       waits=(["v"] if jj == 0 else []))
        S.emit("vector", lambda e: e.tensor_copy(
            out=gL[:, :, :],
            in_=bass.AP(pbS, 0, [[512, 128], [NE, 16], [1, NE]])),
            inc="v", waits=["t"])
        if debug:
            S.emit("vector", lambda e: e.tensor_copy(
                out=graw_dbg[:, :, :], in_=gL[:, :, :]), inc="v")

        def col_b(idx):
            return bass.AP(gcols, idx * 16, [[6 * 16, 128], [1, 16], [0, NE]])

        def snap2():
            if debug and SEQI[0] < 16:
                i = SEQI[0]
                SEQI[0] += 1
                S.emit("vector", lambda e, i=i: e.tensor_copy(
                    out=seq_dbg[:, i, :], in_=gcols[:, 2, :]), inc="v")
        for jb in range(4):
            for jj in range(4):
                j = jb * 4 + jj
                S.emit("tensor", MM(pbY[:, jj * 128:(jj + 1) * 128],
                                    Rb[:, j * 128:(j + 1) * 128], IDN[:, :]),
                       inc=("t" if jj == 3 else None),
                       waits=(["v", "a"] if jj == 0 else []))
            S.emit("vector", lambda e, jb=jb: e.tensor_copy(
                out=gcols[:, 2, jb * 4:(jb + 1) * 4],
                in_=bass.AP(pbY, 0, [[512, 128], [128, 4]])), inc="v",
                waits=["t"])

        S.emit("vector", lambda e: e.tensor_tensor(
            out=gL[:, :, :], in0=gL[:, :, :], in1=col_b(2), op=AL.mult), inc="v")
        snap2()
        S.emit("vector", lambda e: e.tensor_tensor(
            out=gL[:, :, :], in0=gL[:, :, :],
            in1=bass.AP(bg_sb, 0, [[NE, 128], [0, 16], [1, NE]]), op=AL.add),
            inc="v")
        snap2()
        if debug:
            S.emit("vector", lambda e: e.tensor_copy(
                out=gfix_dbg[:, :, :], in_=gL[:, :, :]), inc="v")
        S.emit("vector", lambda e: e.tensor_reduce(
            out=gcols[:, 0, :], in_=gL[:, :, :], axis=AX, op=AL.max), inc="v")
        snap2()
        S.emit("vector", lambda e: e.tensor_tensor(
            out=gw1[:, :, :], in0=gL[:, :, :], in1=col_b(0), op=AL.is_equal),
            inc="v")
        snap2()
        S.emit("vector", lambda e: e.scalar_tensor_tensor(
            out=gw1[:, :, :], in0=gw1[:, :, :], scalar=-1e30, in1=gL[:, :, :],
            op0=AL.mult, op1=AL.add), inc="v")
        snap2()
        S.emit("vector", lambda e: e.tensor_reduce(
            out=gcols[:, 1, :], in_=gw1[:, :, :], axis=AX, op=AL.max), inc="v")
        snap2()
        S.emit("vector", lambda e: e.tensor_tensor(
            out=gw1[:, :, :], in0=gL[:, :, :], in1=col_b(0), op=AL.subtract),
            inc="v")
        snap2()
        S.emit("scalar", lambda e: e.activation(
            out=gw1[:, :, :], in_=gw1[:, :, :], func=AF.Exp, scale=1.0),
            inc="a", waits=["v"])
        snap2()
        S.emit("vector", lambda e: e.tensor_tensor(
            out=gcols[:, 3, :], in0=gcols[:, 0, :], in1=gcols[:, 1, :],
            op=AL.subtract), inc="v", waits=["a"])
        snap2()
        S.emit("scalar", lambda e: e.activation(
            out=gcols[:, 3, :], in_=gcols[:, 3, :], func=AF.Sigmoid, scale=1.0),
            inc="a", waits=["v"])
        snap2()
        S.emit("vector", lambda e: e.tensor_tensor(
            out=gL[:, :, :], in0=gL[:, :, :], in1=col_b(1), op=AL.is_ge),
            inc="v", waits=["a"])
        snap2()
        S.emit("vector", lambda e: e.tensor_tensor(
            out=gw1[:, :, :], in0=gw1[:, :, :], in1=gL[:, :, :], op=AL.mult),
            inc="v")
        snap2()
        S.emit("vector", lambda e: e.tensor_scalar(
            out=gcols[:, 5, :], in0=gcols[:, 3, :], scalar1=0.5, scalar2=None,
            op0=AL.mult), inc="v")
        snap2()
        S.emit("vector", lambda e: e.tensor_tensor(
            out=wtm[:, :, :], in0=gw1[:, :, :], in1=col_b(5), op=AL.mult),
            inc="v")
        snap2()
        for jb in range(4):
            for jj in range(4):
                j = jb * 4 + jj
                S.emit("tensor", MM(pbY[:, jj * 128:(jj + 1) * 128],
                                    Rb[:, j * 128:(j + 1) * 128], IDN[:, :]),
                       inc=("t" if jj == 3 else None),
                       waits=(["v", "a"] if jj == 0 else []))
            S.emit("vector", lambda e, jb=jb: e.tensor_copy(
                out=gcols[:, 2, jb * 4:(jb + 1) * 4],
                in_=bass.AP(pbY, 0, [[512, 128], [128, 4]])), inc="v",
                waits=["t"])
        S.emit("vector", lambda e: e.tensor_tensor(
            out=gL[:, :, :], in0=gw1[:, :, :], in1=col_b(5), op=AL.mult),
            inc="v")
        S.emit("vector", lambda e: e.tensor_tensor(
            out=stm[:, :, :], in0=gL[:, :, :], in1=col_b(2), op=AL.mult),
            inc="v")
        for j4 in range(4):
            for jj in range(4):
                j = j4 * 4 + jj
                S.emit("tensor", MM(pbC[0:NE, jj * 128:(jj + 1) * 128],
                                    stm[:, j, :], IDN[:, :]),
                       inc=("t" if jj == 3 else None),
                       waits=(["v"] if jj == 0 else []))
            S.emit("vector", lambda e, j4=j4: e.tensor_copy(
                out=srows[:, j4 * 512:(j4 + 1) * 512],
                in_=pbC[0:NE, 0:512]), inc="v", waits=["t"])


    def emit_ffn_main(xtil, webuf, sbc, wrow, first_waits=()):
        half_done = []
        evac_v = []
        widx = 0
        hidx = 0
        for tq in range(NTQ):
            sl = slice(tq * TQ, (tq + 1) * TQ)
            t_before = S.now("t")
            for jj in range(4):
                j = tq * 4 + jj
                S.emit("tensor", MM(pbC[0:NE, jj * 128:(jj + 1) * 128],
                                    wtm[:, j, :], IDN[:, :]),
                       inc=("t" if jj == 3 else None),
                       waits=(["v", "a"] if jj == 0 else []))
            S.emit("vector", lambda e: e.tensor_copy(
                out=wrow[:, :], in_=pbC[0:NE, 0:512]), inc="v", waits=["t"])
            for e2 in range(NE):
                S.emit("tensor", MM(pbC[:, 0:TQ], ESEL[:, e2, :],
                                    srows[:, sl]),
                       inc="t", waits=["v", "a"])
                S.emit("scalar", lambda e, e2=e2: e.copy(out=sbc[:, e2 % 2, :],
                                                         in_=pbC[:, 0:TQ]),
                       inc="a", waits=["t", "v"])
                S.emit("vector", lambda e, e2=e2, sl=sl: e.tensor_tensor(
                    out=xtil[:, e2, :, :], in0=xc[:, :, sl],
                    in1=bass.AP(sbc, (e2 % 2) * TQ,
                                [[2 * TQ, 128], [0, NF], [1, TQ]]),
                    op=AL.mult), inc="v", waits=["a", t_before])
            vxt = S.now("v")
            for o in range(NE):
                pb = pbA if widx % 2 == 0 else pbB
                for eh in range(4):
                    par = hidx % 2
                    wg_waits = (list(first_waits) if hidx < 2
                                else [half_done[hidx - 2]])
                    S.emit("gpsimd", lambda e, o=o, eh=eh, par=par: e.dma_start(
                        out=webuf[:, par, :, :, :],
                        in_=bass.AP(wet,
                                    o * 128 * NE * 8 * 128 + eh * 2 * 8 * 128,
                                    [[NE * 8 * 128, 128], [8 * 128, 2],
                                     [128, 8], [1, 128]])),
                        inc="swe", inc_by=16, waits=wg_waits)
                    dma_w = S.now("swe")
                    mm_waits = [dma_w, vxt]
                    if widx >= 2 and eh == 0:
                        mm_waits.append(evac_v[widx - 2])
                    first = True
                    for e4 in range(2):
                        e2 = eh * 2 + e4
                        for f in range(NF):
                            is_last = (e4 == 1 and f == NF - 1)
                            S.emit("tensor", MM(pb[:, 0:TQ],
                                                webuf[:, par, e4, f, :],
                                                xtil[:, e2, f, :],
                                                start=(eh == 0 and first),
                                                stop=False),
                                   inc=("t" if is_last else None),
                                   waits=(mm_waits if first else []))
                            first = False
                    half_done.append(S.now("t"))
                    hidx += 1
                S.emit("tensor", MM(pb[:, 0:TQ],
                                    be_bf[:, o * 128:(o + 1) * 128],
                                    wrow[:, :], start=False, stop=True),
                       inc="t")
                S.emit("vector", lambda e, o=o, sl=sl, pb=pb: e.tensor_tensor(
                    out=x_sb[:, o, sl], in0=pb[:, 0:TQ], in1=x_sb[:, o, sl],
                    op=AL.add), inc="v", waits=["t"])
                evac_v.append(S.now("v"))
                widx += 1

    fctx = ExitStack()
    xtil = fctx.enter_context(nc.sbuf_tensor("xtil1", [128, NE, NF, TQ], BF16))
    webuf = fctx.enter_context(nc.sbuf_tensor("webuf1", [128, 2, 2, 8, 128], BF16))
    sbc = fctx.enter_context(nc.sbuf_tensor("sbc1", [128, 2, TQ], BF16))
    wrow = fctx.enter_context(nc.sbuf_tensor("wrow1", [NE, TQ], BF16))
    emit_ln(xtil, NE * NF * TQ)
    emit_gating()
    if debug:
        S.emit("gpsimd", lambda e: e.dma_start(
            out=dbg["d_gc"].ap(), in_=gcols[:, :, :]),
            inc="sst", inc_by=16, waits=["v"])
        S.emit("gpsimd", lambda e: e.dma_start(
            out=dbg["d_gL"].ap(), in_=gw1[:, :, :]),
            inc="sst", inc_by=16, waits=["v"])
        S.emit("gpsimd", lambda e: e.dma_start(
            out=dbg["d_graw"].ap(), in_=graw_dbg[:, :, :]),
            inc="sst", inc_by=16, waits=["v"])
        S.emit("gpsimd", lambda e: e.dma_start(
            out=dbg["d_gfix"].ap(), in_=gfix_dbg[:, :, :]),
            inc="sst", inc_by=16, waits=["v"])
        S.emit("gpsimd", lambda e: e.dma_start(
            out=dbg["d_m1s"].ap(), in_=m1_dbg[:, :]),
            inc="sst", inc_by=16, waits=["v"])
        S.emit("gpsimd", lambda e: e.dma_start(
            out=dbg["d_rbf"].ap(), in_=rrow[0:1, :]),
            inc="sst", inc_by=16, waits=["v"])
        S.emit("gpsimd", lambda e: e.dma_start(
            out=dbg["d_seq"].ap(), in_=seq_dbg[:, :, :]),
            inc="sst", inc_by=16, waits=["v"])
    emit_ffn_main(xtil, webuf, sbc, wrow, first_waits=[lin])
    if debug:
        S.emit("gpsimd", lambda e: e.dma_start(
            out=bass.AP(dbg["d_x1"], 0, [[TC, 128], [128 * TC, NF], [1, TC]]),
            in_=x_sb[:, :, :]), inc="sst", inc_by=16, waits=["v"])
        S.emit("gpsimd", lambda e: e.dma_start(
            out=dbg["d_srows"].ap(), in_=srows[:, :]),
            inc="sst", inc_by=16, waits=["v"])
    barrier1 = [S.now("t"), S.now("v"), S.now("a")]
    fctx.close()

    actx = ExitStack()
    lnctx = ExitStack()
    xsq_a = lnctx.enter_context(nc.sbuf_tensor("xsq_a", [128, NF, TC], BF16))
    emit_ln(xsq_a, NF * TC)
    lnctx.close()
    qfw = actx.enter_context(nc.sbuf_tensor("qfw", [128, 4, GT], BF16))
    kfw = actx.enter_context(nc.sbuf_tensor("kfw", [128, 4, GT], BF16))
    ktw = actx.enter_context(nc.sbuf_tensor("ktw", [128, 4, GC, 128], BF16))
    vtw = actx.enter_context(nc.sbuf_tensor("vtw", [128, 4, GC, 128], BF16))
    yw = actx.enter_context(nc.sbuf_tensor("yw", [128, 2, GT], BF16))
    kraw = actx.enter_context(nc.sbuf_tensor("kraw", [128, GT], BF16))
    vraw = actx.enter_context(nc.sbuf_tensor("vraw", [128, GT], BF16))
    ksq = actx.enter_context(nc.sbuf_tensor("ksq", [128, GT], BF16))
    rnf = actx.enter_context(nc.sbuf_tensor("rnf", [2, GT], F32))
    rnb = actx.enter_context(nc.sbuf_tensor("rnb", [2, GT], BF16))
    Nb = actx.enter_context(nc.sbuf_tensor("Nb", [128, GC, 128], BF16))
    NTb = actx.enter_context(nc.sbuf_tensor("NTb", [128, GC, 128], BF16))
    Pb = actx.enter_context(nc.sbuf_tensor("Pb", [128, 2, GC, 128], BF16))
    PTb = actx.enter_context(nc.sbuf_tensor("PTb", [128, 2, GC, 128], BF16))
    Xb = actx.enter_context(nc.sbuf_tensor("Xb", [128, 2, NH, GC, 128], BF16))
    Gmb = actx.enter_context(nc.sbuf_tensor("Gmb", [128, 2, NH, GC, 128], BF16))
    WTn = actx.enter_context(nc.sbuf_tensor("WTn", [128, 2, GC, 128], BF16))
    C_sb = actx.enter_context(nc.sbuf_tensor("C_sb", [128, NH, 64], BF16))
    MTf = actx.enter_context(nc.sbuf_tensor("MTf", [128, 64], F32))
    MTb = actx.enter_context(nc.sbuf_tensor("MTb", [128, 64], BF16))
    wo_buf = actx.enter_context(nc.sbuf_tensor("wo_buf", [128, 2, NF, 128], BF16))
    x0_dbg = actx.enter_context(nc.sbuf_tensor("x0_dbg", [128, GC, 128], BF16)) if debug else None
    n_dbg = actx.enter_context(nc.sbuf_tensor("n_dbg", [128, GC, 128], BF16)) if debug else None
    xgctx = ExitStack()
    xg = xgctx.enter_context(nc.sbuf_tensor("xg", [128, 2, NF, GT], BF16))

    S.emit("vector", lambda e: e.tensor_tensor(
        out=xc[:, :, :], in0=xc[:, :, :],
        in1=bass.AP(Rb, 0, [[TC, 128], [0, NF], [1, TC]]), op=AL.mult), inc="v")
    if debug:
        S.emit("gpsimd", lambda e: e.dma_start(
            out=bass.AP(dbg["d_xhat"], 0, [[TC, 128], [128 * TC, NF], [1, TC]]),
            in_=xc[:, :, :]), inc="sst", inc_by=16, waits=["v"])
    S.emit("sync", lambda e: e.dma_start(
        out=bass.AP(ag_in, 0, [[TC, 128], [128 * TC, NF], [1, TC]]),
        in_=xc[:, :, :]), inc="sst", inc_by=16, waits=["v"])
    S.emit("gpsimd", lambda e: e.collective_compute(
        "AllGather", AL.bypass, replica_groups=[list(range(NC_))],
        ins=[ag_in.ap().opt()], outs=[ag_out.ap().opt()]),
        inc="scc", waits=["sst"])
    cc_ag = S.now("scc")

    proj_v = {}
    proj_a = {}
    grp_pA = {}
    pb_t = {}
    ydma = {}

    S.emit("vector", lambda e: e.memset(MTf[:, :], 0.0), inc="v")
    S.emit("vector", lambda e: e.memset(MTb[:, :], 0.0), inc="v")
    pb_state = {"prev_yevac": None, "prev_madd": S.now("v")}

    def emit_proj(g):
        wsl = g % 4
        par = g % 2
        csrc = g // 4
        coff = (g % 4) * GT
        waits = [cc_ag]
        if g >= 2:
            waits.append(grp_pA[(g - 2, 1)]["t"])
        S.emit("sync", lambda e, par=par, csrc=csrc, coff=coff: e.dma_start(
            out=xg[:, par, :, :],
            in_=bass.AP(ag_out, csrc * DM * TC + coff,
                        [[TC, 128], [128 * TC, NF], [1, GT]])),
            inc="sxg", inc_by=16, waits=waits)
        dxg = S.now("sxg")
        wguard = []
        if g >= 4:
            wguard.append(pb_t[g - 4])
        for f in range(NF):
            S.emit("tensor", MM(pbA[:, 0:GT], wq_bf[:, f, :], xg[:, par, f, :],
                                start=(f == 0), stop=(f == NF - 1)),
                   inc=("t" if f == NF - 1 else None),
                   waits=([dxg, "v"] + wguard if f == 0 else []))
        S.emit("scalar", lambda e, wsl=wsl: e.copy(out=qfw[:, wsl, :],
                                                   in_=pbA[:, 0:GT]),
               inc="a", waits=["t"])
        qa = S.now("a")
        for f in range(NF):
            S.emit("tensor", MM(pbA[:, 0:GT], wk_bf[:, f, :], xg[:, par, f, :],
                                start=(f == 0), stop=(f == NF - 1)),
                   inc=("t" if f == NF - 1 else None),
                   waits=([qa] if f == 0 else []))
        S.emit("scalar", lambda e: e.copy(out=kraw[:, :], in_=pbA[:, 0:GT]),
               inc="a", waits=["t"])
        ka = S.now("a")
        S.emit("vector", lambda e: e.tensor_tensor(
            out=ksq[:, :], in0=kraw[:, :], in1=kraw[:, :], op=AL.mult),
            inc="v", waits=[ka])
        S.emit("tensor", MM(pbC[0:2, 0:GT], BONES[:, :], ksq[:, :]),
               inc="t", waits=["v", "a"])
        S.emit("vector", lambda e: e.tensor_scalar(
            out=rnf[:, :], in0=pbC[0:2, 0:GT], scalar1=1e-24, scalar2=None,
            op0=AL.max), inc="v", waits=["t"])
        S.emit("scalar", lambda e: e.activation(
            out=rnf[:, :], in_=rnf[:, :], func=AF.Ln, scale=1.0),
            inc="a", waits=["v"])
        S.emit("scalar", lambda e: e.activation(
            out=rnf[:, :], in_=rnf[:, :], func=AF.Exp, scale=-0.5), inc="a")
        S.emit("vector", lambda e: e.tensor_scalar(
            out=rnb[:, :], in0=rnf[:, :], scalar1=ab_sb[:, 4:5], scalar2=None,
            op0=AL.mult), inc="v", waits=["a"])
        S.emit("tensor", MM(pbC[:, 0:GT], HSEL[:, :], rnb[:, :]),
               inc="t", waits=["v"])
        S.emit("vector", lambda e, wsl=wsl: e.tensor_tensor(
            out=kfw[:, wsl, :], in0=kraw[:, :], in1=pbC[:, 0:GT], op=AL.mult),
            inc="v", waits=["t"])
        vkf = S.now("v")
        for f in range(NF):
            S.emit("tensor", MM(pbA[:, 0:GT], wv_bf[:, f, :], xg[:, par, f, :],
                                start=(f == 0), stop=(f == NF - 1)),
                   inc=("t" if f == NF - 1 else None),
                   waits=([ka] if f == 0 else []))
        S.emit("vector", lambda e: e.tensor_scalar(
            out=vraw[:, :], in0=pbA[:, 0:GT], scalar1=vscale[:, 0:1],
            scalar2=None, op0=AL.mult), inc="v", waits=["t"])
        vv = S.now("v")
        for c4 in range(GC):
            S.emit("tensor", MM(pbC[:, c4 * 128:(c4 + 1) * 128],
                                kfw[:, wsl, c4 * 128:(c4 + 1) * 128][:, :]
                                if False else
                                bass.AP(kfw, wsl * GT + c4 * 128,
                                        [[4 * GT, 128], [1, 128]]),
                                IDN[:, :]),
                   inc=("t" if c4 == GC - 1 else None),
                   waits=([vkf] if c4 == 0 else []))
        S.emit("scalar", lambda e, wsl=wsl: e.copy(
            out=ktw[:, wsl, :, :], in_=pbC[:, 0:512]), inc="a", waits=["t"])
        for c4 in range(GC):
            S.emit("tensor", MM(pbC[:, c4 * 128:(c4 + 1) * 128],
                                bass.AP(vraw, c4 * 128, [[GT, 128], [1, 128]]),
                                IDN[:, :]),
                   inc=("t" if c4 == GC - 1 else None),
                   waits=([vv, "a"] if c4 == 0 else []))
        S.emit("scalar", lambda e, wsl=wsl: e.copy(
            out=vtw[:, wsl, :, :], in_=pbC[:, 0:512]), inc="a", waits=["t"])
        proj_v[g] = S.now("v")
        proj_a[g] = S.now("a")

    mlow3 = bass.AP(MLOW, 0, [[128, 128], [0, GC], [1, 128]])
    mup3 = bass.AP(MUP, 0, [[128, 128], [0, GC], [1, 128]])
    ps3 = bass.AP(pbS, 0, [[512, 128], [128, GC], [1, 128]])
    px3 = bass.AP(pbX, 0, [[512, 128], [128, GC], [1, 128]])

    def emit_phaseA(g, h):
        wsl = g % 4
        gpar = g % 2

        def kf(c4):
            return bass.AP(kfw, 64 * h * 4 * GT + wsl * GT + c4 * 128,
                           [[4 * GT, 64], [1, 128]])

        def qf(c4):
            return bass.AP(qfw, 64 * h * 4 * GT + wsl * GT + c4 * 128,
                           [[4 * GT, 64], [1, 128]])

        pw = proj_a[g]
        pv = proj_v[g]
        for c4 in range(GC):
            S.emit("tensor", MM(pbS[:, c4 * 128:(c4 + 1) * 128], kf(c4), kf(c4)),
                   inc=("t" if c4 == GC - 1 else None),
                   waits=([pw, pv, "v"] if c4 == 0 else []))
        tS = S.now("t")
        S.emit("vector", lambda e: e.tensor_tensor(
            out=Nb[:, :, :], in0=ps3, in1=mlow3, op=AL.mult),
            inc="v", waits=[tS])
        S.emit("vector", lambda e: e.tensor_tensor(
            out=NTb[:, :, :], in0=ps3, in1=mup3, op=AL.mult), inc="v")
        vN = S.now("v")
        for c4 in range(GC):
            S.emit("tensor", MM(pbX[:, c4 * 128:(c4 + 1) * 128], kf(c4), qf(c4)),
                   inc=("t" if c4 == GC - 1 else None),
                   waits=([vN] if c4 == 0 else []))
        tG = S.now("t")
        S.emit("vector", lambda e, gpar=gpar, h=h: e.tensor_tensor(
            out=Gmb[:, gpar, h, :, :], in0=px3, in1=mup3, op=AL.mult),
            inc="v", waits=[tG])
        vG = S.now("v")
        for c4 in range(GC):
            S.emit("tensor", MM(pbX[:, c4 * 128:c4 * 128 + 64],
                                NTb[:, c4, :],
                                ktw[:, wsl, c4, 64 * h:64 * h + 64]),
                   waits=([vG] if c4 == 0 else []))
            S.emit("tensor", MM(pbX[:, c4 * 128 + 64:c4 * 128 + 128],
                                NTb[:, c4, :],
                                vtw[:, wsl, c4, 64 * h:64 * h + 64]),
                   inc=("t" if c4 == GC - 1 else None))
        tX0 = S.now("t")
        pxk = bass.AP(pbX, 0, [[512, 128], [128, GC], [1, 64]])
        pxv = bass.AP(pbX, 64, [[512, 128], [128, GC], [1, 64]])
        S.emit("vector", lambda e, gpar=gpar, h=h, wsl=wsl: e.tensor_tensor(
            out=Xb[:, gpar, h, :, 0:64],
            in0=ktw[:, wsl, :, 64 * h:64 * h + 64], in1=pxk,
            op=AL.subtract), inc="v", waits=[tX0])
        S.emit("vector", lambda e, gpar=gpar, h=h, wsl=wsl: e.tensor_tensor(
            out=Xb[:, gpar, h, :, 64:128],
            in0=vtw[:, wsl, :, 64 * h:64 * h + 64], in1=pxv,
            op=AL.subtract), inc="v")
        if debug and g == 0 and h == 0:
            S.emit("vector", lambda e: e.tensor_copy(
                out=x0_dbg[:, :, :], in_=Xb[:, 0, 0, :, :]), inc="v")
            S.emit("vector", lambda e: e.tensor_copy(
                out=n_dbg[:, :, :], in_=Nb[:, :, :]), inc="v")
        vX = S.now("v")
        for li, lvl in enumerate((2, 4, 8, 16, 32, 64)):
            pp = li % 2
            if li == 0:
                src_p = Nb[:, :, :]
                src_pt = NTb[:, :, :]
            else:
                src_p = Pb[:, (li - 1) % 2, :, :]
                src_pt = PTb[:, (li - 1) % 2, :, :]
            last = lvl == 64
            if not last:
                for c4 in range(GC):
                    S.emit("tensor", MM(pbP[:, c4 * 128:(c4 + 1) * 128],
                                        src_pt[:, c4, :], src_p[:, c4, :]),
                           inc=("t" if c4 == GC - 1 else None),
                           waits=([vX, "a"] if (li == 0 and c4 == 0) else []))
            for c4 in range(GC):
                S.emit("tensor", MM(pbPT[:, c4 * 128:(c4 + 1) * 128],
                                    src_p[:, c4, :], src_pt[:, c4, :]),
                       inc=("t" if c4 == GC - 1 else None),
                       waits=([vX, "a"] if (last and li == 0 and c4 == 0)
                              else []))
            tsq = S.now("t")
            if not last:
                S.emit("scalar", lambda e, pp=pp: e.copy(
                    out=Pb[:, pp, :, :], in_=pbP[:, 0:512]),
                    inc="a", waits=[tsq])
            S.emit("scalar", lambda e, pp=pp: e.copy(
                out=PTb[:, pp, :, :], in_=pbPT[:, 0:512]),
                inc="a", waits=[tsq])
            aPT = S.now("a")
            for c4 in range(GC):
                S.emit("tensor", MM(pbX[:, c4 * 128:(c4 + 1) * 128],
                                    PTb[:, pp, c4, :],
                                    Xb[:, gpar, h, c4, :]),
                       inc=("t" if c4 == GC - 1 else None),
                       waits=([aPT, "v"] if c4 == 0 else []))
            tap = S.now("t")
            S.emit("vector", lambda e, gpar=gpar, h=h: e.tensor_tensor(
                out=Xb[:, gpar, h, :, :], in0=Xb[:, gpar, h, :, :], in1=px3,
                op=AL.add), inc="v", waits=[tap])
        vXf = S.now("v")
        for c4 in range(GC):
            S.emit("tensor", MM(pbP[64 * h:64 * h + 64, c4 * 128:(c4 + 1) * 128],
                                Xb[:, gpar, h, c4, 0:64], IDN[:, :]),
                   inc=("t" if c4 == GC - 1 else None),
                   waits=([vXf, "a"] if c4 == 0 else []))
        tWT = S.now("t")
        S.emit("vector", lambda e, gpar=gpar, h=h: e.tensor_scalar(
            out=WTn[64 * h:64 * h + 64, gpar, :, :],
            in0=bass.AP(pbP, 64 * h * 512, [[512, 64], [128, GC], [1, 128]]),
            scalar1=-1.0, scalar2=None, op0=AL.mult), inc="v", waits=[tWT])
        grp_pA[(g, h)] = {"t": S.now("t"), "v": S.now("v"), "a": S.now("a")}

    def emit_phaseB(c, h, extra_y_wait=None):
        g = c // GC
        wsl = g % 4
        gpar = g % 2
        c4 = c % GC
        hs = slice(64 * h, 64 * h + 64)
        dep = grp_pA[(g, h)]
        qf = bass.AP(qfw, 64 * h * 4 * GT + wsl * GT + c4 * 128,
                     [[4 * GT, 64], [1, 128]])
        w1 = [dep["v"], dep["a"], pb_state["prev_madd"]]
        S.emit("tensor", MM(pbB[:, 0:64], WTn[hs, gpar, c4, :], MTb[hs, :],
                            start=True, stop=False), waits=w1)
        S.emit("tensor", MM(pbB[:, 0:64], IDN[:, :], Xb[:, gpar, h, c4, 64:128],
                            start=False, stop=True), inc="t")
        tC = S.now("t")
        S.emit("scalar", lambda e, h=h: e.copy(out=C_sb[:, h, :],
                                               in_=pbB[:, 0:64]),
               inc="a", waits=[tC])
        aC = S.now("a")
        w2 = [aC]
        if pb_state["prev_yevac"] is not None:
            w2.append(pb_state["prev_yevac"])
        S.emit("tensor", MM(pbY[hs, 0:128], MTb[hs, :], qf,
                            start=True, stop=False), waits=w2)
        S.emit("tensor", MM(pbY[hs, 0:128], C_sb[:, h, :],
                            Gmb[:, gpar, h, c4, :], start=False, stop=True))
        S.emit("tensor", MM(pbB[hs, 64:128],
                            ktw[:, wsl, c4, 64 * h:64 * h + 64],
                            C_sb[:, h, :]), inc="t")
        tY = S.now("t")
        ysl = g % 2
        ywaits = [tY]
        if extra_y_wait is not None:
            ywaits.append(extra_y_wait)
        S.emit("scalar", lambda e, h=h, ysl=ysl, c4=c4, hs=hs: e.copy(
            out=yw[hs, ysl, c4 * 128:(c4 + 1) * 128],
            in_=pbY[hs, 0:128]), inc="a", waits=ywaits)
        pb_state["prev_yevac"] = S.now("a")
        S.emit("vector", lambda e, hs=hs: e.tensor_tensor(
            out=MTf[hs, :], in0=MTf[hs, :], in1=pbB[hs, 64:128],
            op=AL.add), inc="v", waits=[tY])
        S.emit("vector", lambda e, hs=hs: e.tensor_copy(out=MTb[hs, :],
                                                        in_=MTf[hs, :]), inc="v")
        pb_state["prev_madd"] = S.now("v")

    S.emit("vector", lambda e: e.memset(rnf[:, :], 1.0), inc="v")
    S.emit("vector", lambda e: e.tensor_scalar(
        out=rnb[:, :], in0=rnf[:, :], scalar1=ab_sb[:, 7:8], scalar2=None,
        op0=AL.mult), inc="v")
    S.emit("tensor", MM(pbC[:, 0:GT], HSEL[:, :], rnb[:, :]),
           inc="t", waits=["v", "g"])
    S.emit("vector", lambda e: e.tensor_copy(out=vscale[:, :], in_=pbC[:, 0:4]),
           inc="v", waits=["t"])
    emit_proj(0)
    if debug:
        S.emit("gpsimd", lambda e: e.dma_start(
            out=dbg["d_ab"].ap(), in_=ab_sb[:, :]),
            inc="sst", inc_by=16, waits=["v", "a"])
        S.emit("gpsimd", lambda e: e.dma_start(
            out=dbg["d_vs"].ap(), in_=vscale[:, 0:1]),
            inc="sst", inc_by=16, waits=["v", "a"])
    if debug:
        S.emit("gpsimd", lambda e: e.dma_start(
            out=dbg["d_kf"].ap(), in_=kfw[:, 0, :]),
            inc="sst", inc_by=16, waits=[proj_v[0], proj_a[0]])
        S.emit("gpsimd", lambda e: e.dma_start(
            out=dbg["d_kt"].ap(),
            in_=bass.AP(ktw, 0, [[4 * GC * 128, 128], [128, GC], [1, 128]])),
            inc="sst", inc_by=16, waits=[proj_v[0], proj_a[0]])
    for g in range(NG):
        if g + 1 < NG:
            emit_proj(g + 1)
        emit_phaseA(g, 0)
        if debug and g == 0:
            S.emit("gpsimd", lambda e: e.dma_start(
                out=dbg["d_X"].ap(),
                in_=bass.AP(Xb, 0, [[2 * NH * GC * 128, 128], [128, GC],
                                    [1, 128]])),
                inc="sst", inc_by=16, waits=[grp_pA[(0, 0)]["v"]])
            S.emit("gpsimd", lambda e: e.dma_start(
                out=dbg["d_X0"].ap(), in_=x0_dbg[:, :, :]),
                inc="sst", inc_by=16, waits=[grp_pA[(0, 0)]["v"]])
            S.emit("gpsimd", lambda e: e.dma_start(
                out=dbg["d_N"].ap(), in_=n_dbg[:, :, :]),
                inc="sst", inc_by=16, waits=[grp_pA[(0, 0)]["v"]])
        emit_phaseA(g, 1)
        for c4 in range(GC):
            c = g * GC + c4
            eyw = ydma.get(g - 2) if c4 == 0 else None
            emit_phaseB(c, 0, extra_y_wait=eyw)
            emit_phaseB(c, 1)
        pb_t[g] = S.now("t")
        ysl = g % 2
        S.emit("sync", lambda e, g=g, ysl=ysl: e.dma_start(
            out=bass.AP(a2a_in, (g // 4) * 128 * TC + (g % 4) * GT,
                        [[TC, 128], [1, GT]]),
            in_=yw[:, ysl, :]), inc="sst", inc_by=16,
            waits=[pb_state["prev_yevac"]])
        ydma[g] = S.now("sst")

    if debug:
        S.emit("gpsimd", lambda e: e.dma_start(
            out=dbg["d_y"].ap(),
            in_=bass.AP(a2a_in, 0, [[TC, 128], [1, TC]])),
            inc="sst", inc_by=16, waits=["sst"])

    xgctx.close()
    ybuf = actx.enter_context(nc.sbuf_tensor("ybuf", [128, 2, NF, TQ], BF16))

    def a2a_fn(e):
        with allow_ici_all_to_all():
            return e.collective_compute(
                "AllToAll", AL.bypass, replica_groups=[list(range(NC_))],
                ins=[a2a_in.ap().opt()], outs=[a2a_out.ap().opt()])

    S.emit("gpsimd", a2a_fn, inc="scc", waits=["sst"])
    cc_a2a = S.now("scc")

    wo_done = []
    for tq in range(NTQ):
        par = tq % 2
        dwaits = [cc_a2a]
        if tq >= 2:
            dwaits.append(S.now("t"))
        S.emit("sync", lambda e, par=par, tq=tq: e.dma_start(
            out=ybuf[:, par, :, :],
            in_=bass.AP(a2a_out, tq * TQ, [[TC, 128], [128 * TC, NF], [1, TQ]])),
            inc="sxg", inc_by=16, waits=dwaits)
        dy = S.now("sxg")
        sl = slice(tq * TQ, (tq + 1) * TQ)
        for o in range(NE):
            oi = tq * NE + o
            pb = pbA if oi % 2 == 0 else pbB
            wpar = oi % 2
            wo_w = [cc_a2a if oi < 2 else wo_done[oi - 2]]
            S.emit("gpsimd", lambda e, o=o, wpar=wpar: e.dma_start(
                out=wo_buf[:, wpar, :, :],
                in_=bass.AP(wod, o * 128,
                            [[DM, 128], [128 * DM, NF], [1, 128]])),
                inc="swe", inc_by=16, waits=wo_w)
            dwo = S.now("swe")
            for f in range(NF):
                S.emit("tensor", MM(pb[:, 0:TQ],
                                    wo_buf[:, wpar, f, :],
                                    ybuf[:, par, f, :],
                                    start=(f == 0), stop=(f == NF - 1)),
                       inc=("t" if f == NF - 1 else None),
                       waits=([dy, dwo, "v"] if f == 0 else []))
            wo_done.append(S.now("t"))
            S.emit("vector", lambda e, o=o, sl=sl, pb=pb: e.tensor_tensor(
                out=x_sb[:, o, sl], in0=pb[:, 0:TQ], in1=x_sb[:, o, sl],
                op=AL.add), inc="v", waits=["t"])
    if debug:
        S.emit("gpsimd", lambda e: e.dma_start(
            out=bass.AP(dbg["d_x2"], 0, [[TC, 128], [128 * TC, NF], [1, TC]]),
            in_=x_sb[:, :, :]), inc="sst", inc_by=16, waits=["v"])
    barrier2 = [S.now("t"), S.now("v"), S.now("a")]
    actx.close()

    f2 = ExitStack()
    xtil2 = f2.enter_context(nc.sbuf_tensor("xtil2", [128, NE, NF, TQ], BF16))
    webuf2 = f2.enter_context(nc.sbuf_tensor("webuf2", [128, 2, 2, 8, 128], BF16))
    sbc2 = f2.enter_context(nc.sbuf_tensor("sbc2", [128, 2, TQ], BF16))
    wrow2 = f2.enter_context(nc.sbuf_tensor("wrow2", [NE, TQ], BF16))
    S.emit("vector", lambda e: e.tensor_copy(out=mrow[0:1, 0:1],
                                             in_=mrow[0:1, 0:1]),
           inc="v", waits=barrier2)
    emit_ln(xtil2, NE * NF * TQ)
    emit_gating()
    emit_ffn_main(xtil2, webuf2, sbc2, wrow2, first_waits=barrier2)
    f2.close()

    S.emit("sync", lambda e: e.dma_start(
        out=bass.AP(out, 0, [[TC, 128], [128 * TC, NF], [1, TC]]),
        in_=x_sb[:, :, :]), inc="sst", inc_by=16, waits=["v", "a", "t"])

    with nc.Block() as block:
        @block.sync
        def _(eng):
            S.replay("sync", eng)
            eng.wait_ge(S.sems["sst"], S.cnt["sst"])

        @block.gpsimd
        def _(eng):
            S.replay("gpsimd", eng)

        @block.tensor
        def _(eng):
            S.replay("tensor", eng)

        @block.vector
        def _(eng):
            S.replay("vector", eng)

        @block.scalar
        def _(eng):
            S.replay("scalar", eng)

    ctx.close()
    return nc, dbg


_NC_CACHE = {}


def get_nc(debug=False):
    if debug not in _NC_CACHE:
        _NC_CACHE[debug] = build_nc(debug)
    return _NC_CACHE[debug]


def make_in_maps(L0_attn_x, L0_attn_Wq, L0_attn_Wk, L0_attn_Wv, L0_attn_Wo,
                 L0_attn_alpha_raw, L0_attn_beta_raw, L0_ffn_We, L0_ffn_be,
                 L0_ffn_Wg, L0_ffn_bg):
    x = np.ascontiguousarray(np.asarray(L0_attn_x, np.float32)[0])
    wq = np.asarray(L0_attn_Wq, np.float32)
    wk = np.asarray(L0_attn_Wk, np.float32)
    wv = np.asarray(L0_attn_Wv, np.float32)
    wo = np.ascontiguousarray(np.asarray(L0_attn_Wo, np.float32))
    we = np.asarray(L0_ffn_We, np.float32)
    wet = np.ascontiguousarray(
        we.reshape(NE, 8, 128, 8, 128).transpose(3, 2, 0, 1, 4))
    be = np.ascontiguousarray(np.asarray(L0_ffn_be, np.float32))
    wg = np.ascontiguousarray(np.asarray(L0_ffn_Wg, np.float32))
    bg = np.asarray(L0_ffn_bg, np.float32).reshape(1, NE)
    bg128 = np.ascontiguousarray(np.broadcast_to(bg, (128, NE)))
    al = np.asarray(L0_attn_alpha_raw, np.float32).reshape(16)
    bt = np.asarray(L0_attn_beta_raw, np.float32).reshape(16)
    maps = []
    for i in range(NC_):
        tsl = slice(i * TC, (i + 1) * TC)
        hsl = slice(i * 128, (i + 1) * 128)
        maps.append({
            "xT": np.ascontiguousarray(x[tsl].T),
            "wqd": np.ascontiguousarray(wq[:, hsl]),
            "wkd": np.ascontiguousarray(wk[:, hsl]),
            "wvd": np.ascontiguousarray(wv[:, hsl]),
            "wod": wo,
            "wet": wet,
            "bed": be,
            "wgd": wg,
            "bg128": bg128,
            "araw": np.ascontiguousarray(al[2 * i:2 * i + 2].reshape(NH, 1)),
            "braw": np.ascontiguousarray(bt[2 * i:2 * i + 2].reshape(NH, 1)),
        })
    return maps


def kernel(**inputs):
    nc, _ = get_nc(debug=False)
    maps = make_in_maps(**inputs)
    res = run_bass_kernel_spmd(nc, maps, core_ids=list(range(NC_)))
    outs = [res.results[i]["out"].T for i in range(NC_)]
    return np.ascontiguousarray(np.concatenate(outs, axis=0))[None].astype(
        np.float32)
